# version 19
# baseline (speedup 1.0000x reference)
import contextlib
import numpy as np
import ml_dtypes
import concourse.bass as bass
import concourse.mybir as mybir
from concourse.bass_utils import run_bass_kernel_spmd

F32 = mybir.dt.float32
BF16 = mybir.dt.bfloat16
AF = mybir.ActivationFunctionType
ALU = mybir.AluOpType
AX = mybir.AxisListType

BB, SS, EE, HH, GG, DD = 4, 2048, 2048, 16, 4, 128
HID, HIDP, NHC, NEC = 5461, 5504, 43, 16
NK = [16, 14, 12, 10, 8, 6, 4, 2]
NV = [(17 - kc) // 2 for kc in range(16)]
OFF = [0]
for _kc in range(16):
    OFF.append(OFF[-1] + NV[_kc] * 128)
CSM = float(np.float32(1.0 / np.sqrt(128.0)))
WCNT = float(np.float32(0.5 / (HID * EE)))
INVN = float(np.float32(1.0 / 2048.0))

DEBUG_STAGE = None

A_XKV = 0
A_QFM = 16384
A_OFM = 0
A_KFM = 32768
A_VTM = 40960
A_WK = 49152
A_WV = 57344
A_WQ = 49152
A_PT = 49152
A_MSK = 65536
A_G = 16384
A_W1Q = 60416
A_W3Q = 64512
A_SQD = 49152
A_W2Q = 0
A_W2QS = 58368
A_CBF = 67584
BF16_COLS = 68608

F_IDF = 0
F_BQT = 128
F_BKT = 144
F_ONR = 148
F_ONC = 276
F_M16 = 277
F_EPS = 278
F_UC = 279
F_NUC = 282
F_S3R = 285
F_LN = 288
F_P3 = 296
F_CK = 299
F_NS = 300
F_MF = 16816
F_RC = 304
F_X1 = 432
F_ABS = 432
F_RSB = 12720
F_PAR = 14768
F_QIN = 4528
F_AB2 = 432
F_W1S = 432
F_W3S = 4528
F_SIL = 8624
F_W2S = 16816
F_FSC = 18192
F32_COLS = 18704


def _r_of_slot(v, s):
    k = 7 - s
    return 2 * k + (k & 1) if v == 0 else 2 * k + 1 - (k & 1)


def _build_nc(debug_stage=None):
    nc = bass.Bass("TRN2", target_bir_lowering=False)

    d_xkvT = nc.dram_tensor("xkvT", [128, 32768], F32, kind="ExternalInput").ap()
    d_xqT = nc.dram_tensor("xqT", [128, 16384], F32, kind="ExternalInput").ap()
    d_xq = nc.dram_tensor("xq", [1024, EE], F32, kind="ExternalInput").ap()
    d_wq = nc.dram_tensor("wq_sb", [HH, 128, 2048], F32, kind="ExternalInput").ap()
    d_wk = nc.dram_tensor("wk_sb", [GG, 128, 2048], F32, kind="ExternalInput").ap()
    d_wv = nc.dram_tensor("wv_sb", [128, 8192], F32, kind="ExternalInput").ap()
    d_w1 = nc.dram_tensor("w1_sb", [NHC, 128, 2048], F32, kind="ExternalInput").ap()
    d_w3 = nc.dram_tensor("w3_sb", [NHC, 128, 2048], F32, kind="ExternalInput").ap()
    d_w2 = nc.dram_tensor("w2_sb", [NEC, 128, HIDP], F32, kind="ExternalInput").ap()
    d_cf = nc.dram_tensor("constf", [128, 432], F32, kind="ExternalInput").ap()
    d_bv = nc.dram_tensor("bvf", [1, 512], F32, kind="ExternalInput").ap()
    d_cb = nc.dram_tensor("constb", [128, 768], BF16, kind="ExternalInput").ap()
    d_msk = nc.dram_tensor("masks", [128, 2048], BF16, kind="ExternalInput").ap()
    d_out = nc.dram_tensor("out", [1024, EE], F32, kind="ExternalOutput").ap()
    d_sp = nc.dram_tensor("x1spill", [1024, EE], F32).ap()
    d_w2q = nc.dram_tensor("w2qs", [NEC, 128, HIDP], BF16).ap()

    cnt = {}

    with contextlib.ExitStack() as ctx:
        ab = ctx.enter_context(nc.sbuf_tensor("ab", [128, BF16_COLS], BF16))
        af = ctx.enter_context(nc.sbuf_tensor("af", [128, F32_COLS], F32))

        sems = {}
        for name in ["ld", "lg", "la", "lb", "st", "lc", "ls",
                     "lva", "lvb", "lvc", "lvd", "lwk", "lwv",
                     "lx0", "lx1", "lx2", "lx3", "lx4", "lx5", "lx6", "lx7",
                     "lq5", "lqa", "lqb", "lqc", "lqd",
                     "fo", "fx", "fT", "fS", "fV",
                     "sc", "ex", "mk", "av", "rev", "rc", "oev",
                     "pm", "pe", "fq", "fm", "fg", "sil",
                     "bq", "bs", "bm", "be", "bt"]:
            cnt[name] = 0
            sems[name] = ctx.enter_context(nc.semaphore(name))

        def INC(instr, name, k=1):
            instr.then_inc(sems[name], k)
            cnt[name] += k
            return cnt[name]

        def W(eng, name, val):
            if val > 0:
                eng.wait_ge(sems[name], val)

        blk = ctx.enter_context(nc.Block())

        def on(engname, fn):
            getattr(blk, engname)(fn)

        def abf(c0, n):
            return ab[:, c0:c0 + n]

        def aff(c0, n):
            return af[:, c0:c0 + n]

        identf = aff(F_IDF, 128)
        identb = abf(A_CBF, 128)
        ones128 = abf(A_CBF + 128, 128)

        onesrow = af[0:1, F_ONR:F_ONR + 128]
        onescol = aff(F_ONC, 1)
        m16col = aff(F_M16, 1)
        epscol = aff(F_EPS, 1)

        def p0_sync(sync):
            INC(sync.dma_start(out=aff(0, 432), in_=d_cf), "ld", 16)
            INC(sync.dma_start(out=ab[:, A_CBF:A_CBF + 768], in_=d_cb), "ld", 16)
            INC(sync.dma_start(out=abf(A_MSK, 2048), in_=d_msk), "ld", 16)
            INC(sync.dma_start(out=af[0:1, F_FSC:F_FSC + 512], in_=d_bv), "ld", 16)
        on("sync", p0_sync)
        LD_CONSTS = cnt["ld"]

        def ckpt(sync):
            sync.wait_ge(sems["ld"], cnt["ld"])
            INC(sync.dma_start(out=af[0:1, F_CK:F_CK + 1], in_=d_cf[0:1, 0:1]),
                "lc", 16)

        def p0_ck(sync):
            ckpt(sync)
        on("sync", p0_ck)
        LC_CONSTS = cnt["lc"]

        def p0_gp(gpsimd):
            wk_ap = bass.AP(d_wk.tensor, 0, [[2048, 128], [128 * 2048, GG], [1, 2048]])
            INC(gpsimd.dma_start(out=abf(A_WK, 8192), in_=wk_ap), "lwk", 16)
            INC(gpsimd.dma_start(out=abf(A_WV, 8192), in_=d_wv), "lwv", 16)
            for j in range(8):
                INC(gpsimd.dma_start(out=abf(A_XKV + j * 4096, 4096),
                                     in_=d_xkvT[:, j * 4096:(j + 1) * 4096]),
                    "lx%d" % j, 16)
        on("gpsimd", p0_gp)
        LG_KV0 = cnt["lg"]

        ps_stack = contextlib.ExitStack()
        psA = ps_stack.enter_context(nc.psum_tensor("psA", [128, 1024], F32))
        psB = ps_stack.enter_context(nc.psum_tensor("psB", [128, 1024], F32))
        psR = ps_stack.enter_context(nc.psum_tensor("psR", [128, 1024], F32))
        psO = ps_stack.enter_context(nc.psum_tensor("psO", [128, 1024], F32))
        psQK = [psA, psB, psR, psO]

        unit = [0]

        def proj_unit(mm_fn, ev_fn):
            u = unit[0]
            unit[0] += 1

            def tfun(tensor, u=u, mm_fn=mm_fn):
                W(tensor, "pe", max(0, u - 3))
                last = mm_fn(tensor)
                INC(last, "pm", 1)
            on("tensor", tfun)

            def vfun(vector, u=u, ev_fn=ev_fn):
                if u == 0:
                    W(vector, "lc", LC_CONSTS)
                W(vector, "pm", u + 1)
                last = ev_fn(vector)
                INC(last, "pe", 1)
            on("vector", vfun)

        for g in range(GG):
            for half in range(2):
                ps = psQK[unit[0] % 4]

                def mk_mm(g=g, half=half, ps=ps, u0=unit[0]):
                    def mm(tensor):
                        if u0 < 2:
                            W(tensor, "lwk", 16)
                        elif u0 == 2:
                            for s in ["lwk", "lwv", "lx0", "lx1", "lx2", "lx3",
                                      "lx4", "lx5", "lx6", "lx7"]:
                                W(tensor, s, 16)
                        last = None
                        for ec in range(NEC):
                            if u0 < 2:
                                W(tensor, "lx%d" % (ec // 2), 16)
                            lhs = abf(A_WK + g * 2048 + ec * 128, 128)
                            for b2 in range(2):
                                last = tensor.matmul(
                                    ps[:, b2 * 512:(b2 + 1) * 512], lhs,
                                    abf(A_XKV + ec * 2048 + half * 1024 + b2 * 512,
                                        512),
                                    start=(ec == 0), stop=(ec == NEC - 1))
                        return last
                    return mm

                def mk_ev(g=g, ps=ps, half=half):
                    def ev(vector):
                        return vector.tensor_scalar(
                            abf(A_KFM + g * 2048 + half * 1024, 1024), ps[:],
                            aff(F_BKT + g, 1), CSM, ALU.add, ALU.mult)
                    return ev
                proj_unit(mk_mm(), mk_ev())

        for tt in range(16):
            ps = psQK[unit[0] % 4]

            def mk_mm(tt=tt, ps=ps, first=(tt == 0)):
                def mm(tensor):
                    if first:
                        W(tensor, "ld", LD_CONSTS)
                    tensor.matmul(ps[:, 0:512], onesrow,
                                  af[0:1, F_FSC:F_FSC + 512],
                                  start=True, stop=False)
                    last = None
                    for ec in range(NEC):
                        lhs = abf(A_XKV + ec * 2048 + tt * 128, 128)
                        last = tensor.matmul(ps[:, 0:512], lhs,
                                             abf(A_WV + ec * 512, 512),
                                             start=False, stop=(ec == NEC - 1))
                    return last
                return mm

            def mk_ev(tt=tt, ps=ps):
                def ev(vector):
                    return vector.tensor_copy(abf(A_VTM + tt * 512, 512),
                                              ps[:, 0:512])
                return ev
            proj_unit(mk_mm(), mk_ev())

        LG_XQ = LG_KV0 + 4 * 16

        def p_gp2(gpsimd):
            gpsimd.wait_ge(sems["pm"], 24)
            for j in range(4):
                INC(gpsimd.dma_start(out=abf(A_XKV + j * 4096, 4096),
                                     in_=d_xqT[:, j * 4096:(j + 1) * 4096]),
                    "lg", 16)
            for h in range(HH):
                if h >= 2:
                    gpsimd.wait_ge(sems["pm"], 24 + (h - 1))
                INC(gpsimd.dma_start(out=abf(A_WQ + (h % 2) * 2048, 2048),
                                     in_=d_wq[h]), "lg", 16)
        on("gpsimd", p_gp2)

        _mini = {}

        def emit_mini_finalize():
            def p_mini_v1(vector):
                W(vector, "fS", NEC)
                INC(vector.reduce_sum(aff(F_MF + 3, 1), aff(F_PAR + 0, NEC), AX.X),
                    "fq", 1)
            on("vector", p_mini_v1)

            def p_mini_t1(tensor):
                W(tensor, "pe", 31)
                W(tensor, "fq", 1)
                INC(tensor.matmul(psR[0:1, 0:1], onescol, aff(F_MF + 3, 1),
                                  start=True, stop=True), "fT", 1)
            on("tensor", p_mini_t1)
            FT_MF1 = cnt["fT"]

            def p_mini_v2(vector):
                W(vector, "fT", FT_MF1)
                INC(vector.tensor_scalar(af[0:1, F_MF:F_MF + 1], psR[0:1, 0:1],
                                         WCNT, None, ALU.mult), "fq", 1)
            on("vector", p_mini_v2)

            def p_mini_t2(tensor):
                W(tensor, "fq", 2)
                INC(tensor.matmul(psR[:, 4:5], onesrow[:, 0:128],
                                  af[0:1, F_MF:F_MF + 1], start=True, stop=True),
                    "fT", 1)
            on("tensor", p_mini_t2)
            FT_MF2 = cnt["fT"]
            _mini["FT_MF2"] = FT_MF2

            def p_mini_v3(vector):
                W(vector, "fT", FT_MF2)
                vector.tensor_copy(aff(F_MF + 1, 1), psR[:, 4:5])
                INC(vector.tensor_scalar(aff(F_MF + 2, 1), psR[:, 4:5], -1.0, None,
                                         ALU.mult), "fq", 1)
            on("vector", p_mini_v3)
            pass
            return cnt["fq"]

        FQ_MF = 3
        for h in range(HH):
            if h == 7:
                emit_mini_finalize()
            ps = psQK[unit[0] % 4]
            lg_wq_h = LG_XQ + (h + 1) * 16

            def mk_mm(h=h, ps=ps, lg_wq_h=lg_wq_h):
                def mm(tensor):
                    W(tensor, "lg", LG_XQ + 16 * min(h + 2, HH))
                    if h == 10:
                        W(tensor, "fq", FQ_MF)
                    last = None
                    for ec in range(NEC):
                        lhs = abf(A_WQ + (h % 2) * 2048 + ec * 128, 128)
                        for b2 in range(2):
                            last = tensor.matmul(
                                ps[:, b2 * 512:(b2 + 1) * 512], lhs,
                                abf(A_XKV + ec * 1024 + b2 * 512, 512),
                                start=(ec == 0), stop=(ec == NEC - 1))
                    return last
                return mm

            def mk_ev(h=h, ps=ps):
                def ev(vector):
                    return vector.tensor_scalar(
                        abf(A_QFM + h * 1024, 1024), ps[:],
                        aff(F_BQT + h, 1), None, ALU.add)
                return ev
            proj_unit(mk_mm(), mk_ev())

        NUNITS = unit[0]

        ab2_pieces = [("w1", hc) for hc in range(NHC)] +                      [("w3", hc) for hc in range(NHC)]
        NAB2 = len(ab2_pieces)

        ph1_ld = {}

        xq_ld = {}

        def p_abs_sync(sync):
            W(sync, "ld", LD_CONSTS)
            for i in range(1, NEC, 2):
                if i >= 3:
                    W(sync, "fS", i - 1)
                ph1_ld[i] = INC(sync.dma_start(out=aff(F_ABS + 6144, HIDP),
                                               in_=d_w2[i]), "ld", 16)
            W(sync, "fS", NEC)
            INC(sync.dma_start(out=aff(F_X1 + 5 * 2048, 2048),
                               in_=d_xq[5 * 128:6 * 128, :]), "lq5", 16)
        on("sync", p_abs_sync)

        def p_abs_scalar(scalar):
            INC(scalar.dma_start(out=aff(F_ABS, HIDP), in_=d_w2[0]), "ls", 16)
            for i in range(NEC):
                if i % 2 == 0:
                    W(scalar, "ls", 16 * (i // 2 + 1))
                else:
                    W(scalar, "ld", ph1_ld[i])
                buf = aff(F_ABS + (i % 2) * 6144, HIDP)
                INC(scalar.activation(buf, buf, AF.Abs,
                                      accum_out=aff(F_PAR + i, 1)), "fS", 1)
                if i % 2 == 0 and i + 2 < NEC:
                    W(scalar, "fS", i + 1)
                    INC(scalar.dma_start(out=aff(F_ABS, HIDP),
                                         in_=d_w2[i + 2]), "ls", 16)
        on("scalar", p_abs_scalar)
        FS_PRE = cnt["fS"]


        def p_ab2_gp(gpsimd):
            W(gpsimd, "fS", FS_PRE)
            for j in range(NAB2):
                w, hc = ab2_pieces[j]
                dsrc = d_w1 if w == "w1" else d_w3
                if j >= 2:
                    W(gpsimd, "fS", FS_PRE + j - 1)
                INC(gpsimd.dma_start(out=aff(F_AB2 + (j % 2) * 2048, 2048),
                                     in_=dsrc[hc]), "la" if j % 2 == 0 else "lb", 16)
        on("gpsimd", p_ab2_gp)

        def emit_ab2_scalar(j):
            def sfun(scalar, j=j):
                W(scalar, "la" if j % 2 == 0 else "lb", 16 * (j // 2 + 1))
                buf = aff(F_AB2 + (j % 2) * 2048, 2048)
                INC(scalar.activation(buf, buf, AF.Abs,
                                      accum_out=aff(F_PAR + NEC + j, 1)), "fS", 1)
            on("scalar", sfun)



        w2p_ld = {}

        HQ = HIDP // 4

        def p_w2p_sync(sync):
            for ec in range(1, NEC + 1):
                W(sync, "fq", FQ_MF + 2 * (4 * ec))
                w2p_ld[("out", ec - 1)] = INC(
                    sync.dma_start(out=d_w2q[ec - 1], in_=abf(A_W2QS, HIDP)),
                    "ld", 16)
        on("sync", p_w2p_sync)

        NQUAR = 4 * NEC
        QSEM = ["lva", "lvb", "lvc", "lvd"]

        def emit_qin_push(q):
            def sfun(scalar, q=q):
                ec, quar = divmod(q, 4)
                if q < 4:
                    W(scalar, "fS", NEC)
                else:
                    W(scalar, "fq", FQ_MF + 2 * (q - 3))
                INC(scalar.dma_start(out=aff(F_QIN + (q % 4) * HQ, HQ),
                                     in_=d_w2[ec][:, quar * HQ:(quar + 1) * HQ]),
                    QSEM[q % 4], 16)
            on("scalar", sfun)

        def emit_w2p_quarter(q):
            ec, quar = divmod(q, 4)

            def vfun(vector, q=q, ec=ec, quar=quar):
                if q == 0:
                    W(vector, "fq", FQ_MF)
                    W(vector, "pm", 24)
                if quar == 0 and ec >= 1:
                    W(vector, "ld", w2p_ld[("out", ec - 1)])
                W(vector, QSEM[q % 4], 16 * (q // 4 + 1))
                st = aff(F_QIN + (q % 4) * HQ, HQ)
                qq = abf(A_W2QS + quar * HQ, HQ)
                n1 = INC(vector.tensor_scalar(qq, st, aff(F_MF + 1, 1), None,
                                              ALU.is_gt), "fq", 1)
                W(vector, "fq", n1)
                INC(vector.scalar_tensor_tensor(qq, st, aff(F_MF + 2, 1), qq,
                                                ALU.is_lt, ALU.subtract), "fq", 1)
            on("vector", vfun)

        ALAG = 3

        def p_att_tensor(tensor):
            W(tensor, "pe", NUNITS)
            W(tensor, "lc", LC_CONSTS)
            for idx in range(256 + ALAG):
                if idx < 256:
                    h, kc = divmod(idx, 16)
                    N = NV[kc] * 128
                    g = h // 4
                    ps = psA if kc % 2 == 0 else psB
                    W(tensor, "ex", idx - 1)
                    last = None
                    for b2 in range((N + 511) // 512):
                        n2 = min(512, N - b2 * 512)
                        last = tensor.matmul(
                            ps[:, b2 * 512:b2 * 512 + n2],
                            abf(A_KFM + g * 2048 + kc * 128, 128),
                            abf(A_QFM + h * 1024 + b2 * 512, n2),
                            start=True, stop=True)
                    INC(last, "sc", 1)
                if idx >= ALAG:
                    jdx = idx - ALAG
                    h, kc = divmod(jdx, 16)
                    N = NV[kc] * 128
                    g = h // 4
                    W(tensor, "mk", jdx + 1)
                    if jdx == 0:
                        W(tensor, "fq", FQ_MF)
                    if kc == 0:
                        W(tensor, "rev", h)
                        W(tensor, "rc", h)
                        W(tensor, "oev", h)
                    for b2 in range((N + 511) // 512):
                        n2 = min(512, N - b2 * 512)
                        stop = (kc == 15) if b2 == 0 else (kc == 7)
                        tensor.matmul(psR[:, b2 * 512:b2 * 512 + n2], ones128,
                                      abf(A_PT + OFF[kc] + b2 * 512, n2),
                                      start=(kc == 0), stop=stop)
                    last = None
                    for b2 in range((N + 511) // 512):
                        n2 = min(512, N - b2 * 512)
                        stop = (kc == 15) if b2 == 0 else (kc == 7)
                        last = tensor.matmul(
                            psO[:, b2 * 512:b2 * 512 + n2],
                            abf(A_VTM + kc * 512 + g * 128, 128),
                            abf(A_PT + OFF[kc] + b2 * 512, n2),
                            start=(kc == 0), stop=stop)
                    INC(last, "av", 1)
                    if kc == 15:
                        W(tensor, "rev", h + 1)
                        last = None
                        for j in range(8):
                            last = tensor.transpose(
                                psR[:, j * 128:(j + 1) * 128],
                                aff(F_RSB + (h % 2) * 1024 + j * 128, 128), identf)
                        INC(last, "fT", 1)
        on("tensor", p_att_tensor)
        FT_ATT = cnt["fT"]

        def p_att_scalar_idx(idx):
            def sfun(scalar, idx=idx):
                if idx < 256:
                    h, kc = divmod(idx, 16)
                    N = NV[kc] * 128
                    ps = psA if kc % 2 == 0 else psB
                    W(scalar, "sc", idx + 1)
                    if idx >= 16:
                        W(scalar, "av", idx - 15)
                    INC(scalar.activation(abf(A_PT + OFF[kc], N), ps[:, 0:N],
                                          AF.Exp, bias=m16col), "ex", 1)
                tidx = idx - ALAG
                if tidx >= 0 and tidx % 16 == 15:
                    h2 = tidx // 16
                    W(scalar, "av", tidx + 1)
                    INC(scalar.activation(abf(A_OFM + h2 * 1024, 1024), psO[:],
                                          AF.Copy), "oev", 1)
            on("scalar", sfun)

        _nab = [0]
        for q0 in range(4):
            emit_qin_push(q0)
        for idx in range(256 + ALAG):
            p_att_scalar_idx(idx)
            if idx % 4 == 0 and idx >= 8 and idx // 4 + 2 < NQUAR:
                emit_qin_push(idx // 4 + 2)
            want = min(((idx + 1) * NAB2) // 256, NAB2)
            while _nab[0] < want:
                emit_ab2_scalar(_nab[0])
                _nab[0] += 1
        while _nab[0] < NAB2:
            emit_ab2_scalar(_nab[0])
            _nab[0] += 1
        FS_ABS = FS_PRE + NAB2

        def p_att_vector_idx(idx):
            def vfun(vector, idx=idx):
                if idx == 0:
                    W(vector, "lc", LC_CONSTS)
                if idx < 256:
                    h, kc = divmod(idx, 16)
                    nv = NV[kc]
                    W(vector, "ex", idx + 1)
                    pslab = abf(A_PT + OFF[kc] + (nv - 1) * 128, 128)
                    INC(vector.tensor_tensor(pslab, pslab,
                                             abf(A_MSK + kc * 128, 128), ALU.mult),
                        "mk", 1)
                tidx = idx - ALAG
                if tidx >= 0 and tidx % 16 == 15:
                    h2 = tidx // 16
                    W(vector, "av", tidx + 1)
                    INC(vector.tensor_copy(aff(F_RSB + (h2 % 2) * 1024, 1024),
                                           psR[:]), "rev", 1)
                    W(vector, "fT", _mini["FT_MF2"] + h2 + 1)
                    last = None
                    for j in range(8):
                        last = vector.reciprocal(aff(F_RC + h2 * 8 + j, 1),
                                                 psR[:, j * 128:j * 128 + 1])
                    INC(last, "rc", 1)
            on("vector", vfun)

        for idx in range(256 + ALAG):
            p_att_vector_idx(idx)
            if idx % 4 == 2 and idx // 4 < NQUAR:
                emit_w2p_quarter(idx // 4)
        FQ_W2END = cnt["fq"]

        def p_fin_vector(vector):
            W(vector, "fS", FS_ABS)
            vector.reduce_sum(aff(F_P3 + 0, 1), aff(F_PAR + NEC, NHC), AX.X)
            vector.reduce_sum(aff(F_P3 + 1, 1), aff(F_PAR + NEC + NHC, NHC), AX.X)
            INC(vector.reduce_sum(aff(F_P3 + 2, 1), aff(F_PAR + 0, NEC), AX.X),
                "fV", 1)
        on("vector", p_fin_vector)
        FV_P3 = cnt["fV"]

        def p_fin_tensor(tensor):
            W(tensor, "fV", FV_P3)
            W(tensor, "ex", 256)
            INC(tensor.matmul(psB[0:1, 0:3], onescol, aff(F_P3, 3),
                              start=True, stop=True), "fT", 1)
        on("tensor", p_fin_tensor)
        FT_S3 = cnt["fT"]

        def p_fin_vector2(vector):
            W(vector, "fT", FT_S3)
            INC(vector.tensor_scalar(af[0:1, F_S3R:F_S3R + 3], psB[0:1, 0:3],
                                     WCNT, None, ALU.mult), "fV", 1)
        on("vector", p_fin_vector2)
        FV_S3 = cnt["fV"]

        def p_fin_tensor2(tensor):
            W(tensor, "fV", FV_S3)
            INC(tensor.matmul(psB[:, 4:7], onesrow, af[0:1, F_S3R:F_S3R + 3],
                              start=True, stop=True), "fT", 1)
        on("tensor", p_fin_tensor2)
        FT_REP = cnt["fT"]

        def p_fin_vector3(vector):
            W(vector, "fT", FT_REP)
            vector.tensor_copy(aff(F_UC, 3), psB[:, 4:7])
            vector.tensor_scalar(aff(F_NUC, 3), psB[:, 4:7], -1.0, None, ALU.mult)
            INC(vector.tensor_scalar(aff(F_NS, 3), psB[:, 4:7], -2.0, None,
                                     ALU.mult), "fV", 1)
        on("vector", p_fin_vector3)
        FV_U = cnt["fV"]

        JORD = [5, 2, 3, 4, 0, 1, 6, 7]

        XQW = {5: ("lq5", 16), 2: ("lqa", 48), 3: ("lqa", 48), 4: ("lqa", 48),
               0: ("lqb", 32), 1: ("lqb", 32), 6: ("lqc", 16), 7: ("lqd", 16)}

        def p_attb_sync(sync):
            W(sync, "fq", FQ_W2END)
            for t in (2, 3, 4):
                INC(sync.dma_start(out=aff(F_X1 + t * 2048, 2048),
                                   in_=d_xq[t * 128:(t + 1) * 128, :]), "lqa", 16)
            W(sync, "fS", FS_ABS)
            for t in (0, 1):
                INC(sync.dma_start(out=aff(F_X1 + t * 2048, 2048),
                                   in_=d_xq[t * 128:(t + 1) * 128, :]), "lqb", 16)
            W(sync, "fT", FT_ATT)
            INC(sync.dma_start(out=aff(F_X1 + 6 * 2048, 2048),
                               in_=d_xq[6 * 128:7 * 128, :]), "lqc", 16)
            W(sync, "fV", FV_U)
            INC(sync.dma_start(out=aff(F_X1 + 7 * 2048, 2048),
                               in_=d_xq[7 * 128:8 * 128, :]), "lqd", 16)
        on("sync", p_attb_sync)

        ps_stack.close()
        ps_stack2 = contextlib.ExitStack()
        pOT = [ps_stack2.enter_context(nc.psum_tensor("pOT%d" % i, [128, 128], BF16))
               for i in range(2)]

        _pi = [0]
        for jpos, j in enumerate(JORD):
            def tfun(tensor, j=j, first=(jpos == 0)):
                if first:
                    W(tensor, "oev", 16)
                    W(tensor, "rc", 16)
                for h in range(HH):
                    pi = _pi[0]
                    if pi >= 2:
                        W(tensor, "fo", pi - 1)
                    INC(tensor.transpose(pOT[pi % 2][:],
                                         abf(A_OFM + h * 1024 + j * 128, 128),
                                         identb), "fT", 1)
                    _pi[0] += 1
            on("tensor", tfun)

            def vfun(vector, j=j, pi0=jpos * HH):
                W(vector, XQW[j][0], XQW[j][1])
                for h in range(HH):
                    pi = pi0 + h
                    W(vector, "fT", FT_REP + pi + 1)
                    m = h * 8 + j
                    x1s = aff(F_X1 + j * 2048 + h * 128, 128)
                    INC(vector.scalar_tensor_tensor(
                        x1s, pOT[pi % 2][:], aff(F_RC + m, 1), x1s,
                        ALU.mult, ALU.add), "fo", 1)
            on("vector", vfun)
        FT_OB = cnt["fT"]

        def emit_store_out(wait_v, per_tile=None):
            def p_st(sync):
                for t in range(8):
                    W(sync, "fV", per_tile[t] if per_tile else wait_v)
                    INC(sync.dma_start(out=d_out[t * 128:(t + 1) * 128, :],
                                       in_=aff(F_X1 + t * 2048, 2048)), "st", 16)
                sync.wait_ge(sems["st"], cnt["st"])
            on("sync", p_st)

        ln_tile_done = []

        def emit_ln(first_wait_v, tiles=None, extra=None):
            del ln_tile_done[:]
            ln_done = {}
            prev_v = [first_wait_v]
            for tpos, t in enumerate(tiles if tiles is not None else range(8)):
                ex_w = extra(tpos) if extra else []

                def sfun(scalar, t=t, pv=prev_v[0], ex_w=ex_w):
                    for s, v in ex_w:
                        W(scalar, s, v)
                    W(scalar, "fV", pv)
                    INC(scalar.activation(abf(A_SQD, 2048), aff(F_X1 + t * 2048, 2048),
                                          AF.Square, accum_out=aff(F_LN + 1, 1)),
                        "fS", 1)
                on("scalar", sfun)

                def vfun(vector, t=t, pv=prev_v[0], ex_w=ex_w):
                    for s, v in ex_w:
                        W(vector, s, v)
                    W(vector, "fV", pv)
                    INC(vector.reduce_sum(aff(F_LN + 0, 1), aff(F_X1 + t * 2048, 2048),
                                          AX.X), "fV", 1)
                on("vector", vfun)
                v_sum = cnt["fV"]

                def sfun2(scalar, v_sum=v_sum):
                    W(scalar, "fV", v_sum)
                    scalar.mul(aff(F_LN + 2, 1), aff(F_LN + 0, 1), INVN)
                    INC(scalar.activation(aff(F_LN + 3, 1), aff(F_LN + 0, 1),
                                          AF.Square, scale=INVN), "fS", 1)
                on("scalar", sfun2)
                s_m = cnt["fS"]

                def vfun2(vector, s_m=s_m):
                    W(vector, "fS", s_m)
                    INC(vector.scalar_tensor_tensor(aff(F_LN + 4, 1), aff(F_LN + 1, 1),
                                                    INVN, aff(F_LN + 3, 1),
                                                    ALU.mult, ALU.subtract), "fV", 1)
                on("vector", vfun2)
                v_var = cnt["fV"]

                def sfun3(scalar, v_var=v_var):
                    W(scalar, "fV", v_var)
                    INC(scalar.activation(aff(F_LN + 5, 1), aff(F_LN + 4, 1),
                                          AF.Sqrt, bias=epscol), "fS", 1)
                on("scalar", sfun3)
                s_std = cnt["fS"]

                def vfun3(vector, t=t, s_std=s_std):
                    W(vector, "fS", s_std)
                    rc_done = INC(vector.reciprocal(aff(F_LN + 6, 1),
                                                    aff(F_LN + 5, 1)), "fV", 1)
                    W(vector, "fV", rc_done)
                    INC(vector.tensor_scalar(aff(F_X1 + t * 2048, 2048),
                                             aff(F_X1 + t * 2048, 2048),
                                             aff(F_LN + 2, 1), aff(F_LN + 6, 1),
                                             ALU.subtract, ALU.mult), "fV", 1)
                on("vector", vfun3)
                prev_v[0] = cnt["fV"]
                ln_tile_done.append(cnt["fV"])
                ln_done[t] = cnt["fV"]
            return ln_done

        ln1_done = emit_ln(cnt["fV"], tiles=JORD,
                           extra=lambda tpos: [("fo", 16 * (tpos + 1))])
        FV_LN1 = cnt["fV"]

        if debug_stage == "x1":
            emit_store_out(FV_LN1)
            ps_stack2.close()
            return nc

        def p_spill_sync(sync):
            for t in JORD:
                W(sync, "fV", ln1_done[t])
                INC(sync.dma_start(out=d_sp[t * 128:(t + 1) * 128, :],
                                   in_=aff(F_X1 + t * 2048, 2048)), "ld", 16)
        on("sync", p_spill_sync)
        LD_SPILL = cnt["ld"]

        ps_stack2.close()
        ps_stack3 = contextlib.ExitStack()
        pH = [ps_stack3.enter_context(nc.psum_tensor("pH%d" % i, [128, 1024], F32))
              for i in range(4)]

        def p_xf_tensor(tensor):
            W(tensor, "fo", 128)
            ti = 0
            for t in JORD:
                W(tensor, "fV", ln1_done[t])
                for ec in range(NEC):
                    if ti >= 8:
                        W(tensor, "fx", ti - 7)
                    slot = pH[ti % 4][:, ((ti // 4) % 2) * 512:((ti // 4) % 2) * 512 + 128]
                    INC(tensor.transpose(slot, aff(F_X1 + t * 2048 + ec * 128, 128),
                                         identf), "fT", 1)
                    ti += 1
        on("tensor", p_xf_tensor)
        FT_XF = cnt["fT"]

        def p_xf_vector(vector):
            ti = 0
            for t in JORD:
                for ec in range(NEC):
                    W(vector, "fT", FT_OB + ti + 1)
                    slot = pH[ti % 4][:, ((ti // 4) % 2) * 512:((ti // 4) % 2) * 512 + 128]
                    INC(vector.tensor_copy(abf(A_XKV + ec * 1024 + t * 128, 128), slot),
                        "fx", 1)
                    ti += 1
        on("vector", p_xf_vector)
        FX_XF = cnt["fx"]

        def p_fa_sync(sync):
            W(sync, "ld", LD_SPILL)
            W(sync, "fT", FT_XF)
            for hc in range(NHC):
                for w, base, dsrc in (("w1", F_W1S, d_w1), ("w3", F_W3S, d_w3)):
                    if hc >= 2:
                        W(sync, "bq", (hc - 2) * 4 + (2 if w == "w1" else 4))
                    INC(sync.dma_start(out=aff(base + (hc % 2) * 2048, 2048),
                                       in_=dsrc[hc]), "ld", 16)
        on("sync", p_fa_sync)

        def ld_fa(hc, w):
            ss = hc * 2 + (0 if w == "w1" else 1)
            return LD_SPILL + 16 * min(ss + 4, NHC * 2)

        def emit_fa_quant(hc):
            def vfun(vector, hc=hc):
                if hc == 0:
                    W(vector, "fV", FV_U)
                    W(vector, "fx", FX_XF)
                for w, sbase, qbase, uc in (("w1", F_W1S, A_W1Q, 0),
                                            ("w3", F_W3S, A_W3Q, 1)):
                    W(vector, "ld", ld_fa(hc, w))
                    if hc >= 2:
                        W(vector, "fm", (hc - 2) * 2 + 2)
                    st = aff(sbase + (hc % 2) * 2048, 2048)
                    qq = abf(qbase + (hc % 2) * 2048, 2048)
                    n1 = INC(vector.tensor_scalar(qq, st, aff(F_UC + uc, 1), None,
                                                  ALU.is_gt), "bq", 1)
                    W(vector, "bq", n1)
                    INC(vector.scalar_tensor_tensor(qq, st, aff(F_NUC + uc, 1), qq,
                                                    ALU.is_lt, ALU.subtract), "bq", 1)
            on("vector", vfun)

        def emit_fa_g(hc):
            def vfun(vector, hc=hc):
                par = hc % 2
                W(vector, "fm", hc * 2 + 2)
                W(vector, "sil", hc + 1)
                INC(vector.scalar_tensor_tensor(
                    abf(A_G + hc * 1024, 1024), pH[2 + par][:],
                    aff(F_NS + 1, 1), aff(F_SIL, 1024),
                    ALU.mult, ALU.mult), "fg", 1)
            on("vector", vfun)

        for hc in range(NHC):
            emit_fa_quant(hc)
            if hc >= 2:
                emit_fa_g(hc - 2)
        BQ_FA = cnt["bq"]

        def p_fa_tensor(tensor):
            W(tensor, "fx", FX_XF)
            for hc in range(NHC):
                par = hc % 2
                for w, qbase, psi in (("w1", A_W1Q, 0), ("w3", A_W3Q, 2)):
                    W(tensor, "bq", hc * 4 + (2 if w == "w1" else 4))
                    if hc >= 2:
                        if w == "w1":
                            W(tensor, "sil", hc - 1)
                        else:
                            W(tensor, "fg", hc - 1)
                    ps = pH[psi + par]
                    last = None
                    for ec in range(NEC):
                        lhs = abf(qbase + par * 2048 + ec * 128, 128)
                        for b2 in range(2):
                            last = tensor.matmul(
                                ps[:, b2 * 512:(b2 + 1) * 512], lhs,
                                abf(A_XKV + ec * 1024 + b2 * 512, 512),
                                start=(ec == 0), stop=(ec == NEC - 1))
                    INC(last, "fm", 1)
        on("tensor", p_fa_tensor)
        FM_FA = cnt["fm"]

        def p_fa_scalar(scalar):
            for hc in range(NHC):
                par = hc % 2
                W(scalar, "fm", hc * 2 + 1)
                if hc >= 1:
                    W(scalar, "fg", hc)
                INC(scalar.activation(aff(F_SIL, 1024), pH[0 + par][:], AF.Silu,
                                      scale=aff(F_NS + 0, 1)), "sil", 1)
        on("scalar", p_fa_scalar)

        emit_fa_g(NHC - 2)
        emit_fa_g(NHC - 1)
        FG_ALL = cnt["fg"]

        if debug_stage == "ffna":
            def p_dbg_sync(sync):
                W(sync, "bq", BQ_FA)
                W(sync, "sil", NHC)
                W(sync, "fg", NHC)
                for t in range(8):
                    INC(sync.dma_start(out=aff(F_X1 + t * 2048, 2048),
                                       in_=d_sp[t * 128:(t + 1) * 128, :]), "ld", 16)
                ckpt(sync)
                for t in range(8):
                    INC(sync.dma_start(out=d_out[t * 128:(t + 1) * 128, :],
                                       in_=aff(F_X1 + t * 2048, 2048)), "st", 16)
                sync.wait_ge(sems["st"], cnt["st"])
            on("sync", p_dbg_sync)
            ps_stack3.close()
            return nc

        def p_fb_sync0(sync):
            W(sync, "bq", BQ_FA)
            W(sync, "sil", NHC)
            W(sync, "fg", NHC)
            for t in range(8):
                INC(sync.dma_start(out=aff(F_X1 + t * 2048, 2048),
                                   in_=d_sp[t * 128:(t + 1) * 128, :]), "ld", 16)
            ckpt(sync)
        on("sync", p_fb_sync0)
        LD_X1R = cnt["ld"]
        LC_X1R = cnt["lc"]

        w2l_ld = {}

        def p_fb_sync(sync):
            W(sync, "fm", FM_FA)
            for ec in range(NEC):
                if ec >= 2:
                    W(sync, "bm", ec - 1)
                w2l_ld[ec] = INC(
                    sync.dma_start(out=ab[:, A_W2Q + (ec % 2) * HIDP:
                                       A_W2Q + (ec % 2) * HIDP + HIDP],
                                   in_=d_w2q[ec]), "ld", 16)
        on("sync", p_fb_sync)



        def emit_fb_stt(ec):
            def vfun(vector, ec=ec):
                if ec == 0:
                    W(vector, "lc", LC_X1R)
                pT = pH[2 + ec % 2]
                W(vector, "bt", 2 * ec + 2)
                last = None
                for t in range(8):
                    x1s = aff(F_X1 + t * 2048 + ec * 128, 128)
                    last = vector.tensor_tensor(x1s, pT[:, t * 128:(t + 1) * 128],
                                                x1s, ALU.add)
                INC(last, "bs", 1)
            on("vector", vfun)

        def emit_fb_evac(ec):
            def sfun(scalar, ec=ec):
                par = ec % 2
                W(scalar, "bm", ec + 1)
                for half in range(2):
                    if half == 0:
                        if ec >= 1:
                            W(scalar, "bt", 2 * ec)
                    else:
                        W(scalar, "bt", 2 * ec + 1)
                    INC(scalar.activation(aff(F_FSC, 512),
                                          pH[par][:, half * 512:(half + 1) * 512],
                                          AF.Copy, scale=aff(F_NS + 2, 1)), "be", 1)
            on("scalar", sfun)

        for ec in range(NEC):
            if ec >= 1:
                emit_fb_evac(ec - 1)
            if ec >= 2:
                emit_fb_stt(ec - 2)

        def p_fb_tensor(tensor):
            W(tensor, "fg", FG_ALL)
            for ec in range(NEC + 1):
                if ec < NEC:
                    par = ec % 2
                    W(tensor, "ld", w2l_ld[min(ec + 1, NEC - 1)])
                    if ec >= 2:
                        W(tensor, "be", (ec - 1) * 2)
                    ps = pH[par]
                    last = None
                    for hc in range(NHC):
                        lhs = abf(A_W2Q + par * HIDP + hc * 128, 128)
                        for b2 in range(2):
                            last = tensor.matmul(
                                ps[:, b2 * 512:(b2 + 1) * 512], lhs,
                                abf(A_G + hc * 1024 + b2 * 512, 512),
                                start=(hc == 0), stop=(hc == NHC - 1))
                    INC(last, "bm", 1)
                if ec >= 1:
                    jc = ec - 1
                    pT = pH[2 + jc % 2]
                    if jc >= 2:
                        W(tensor, "bs", jc - 1)
                    for half in range(2):
                        hh = jc * 2 + half
                        W(tensor, "be", hh + 1)
                        last_t = None
                        for t4 in range(4):
                            t = half * 4 + t4
                            last_t = tensor.transpose(
                                pT[:, t * 128:(t + 1) * 128],
                                aff(F_FSC + t4 * 128, 128), identf)
                        INC(last_t, "bt", 1)
        on("tensor", p_fb_tensor)


        emit_fb_evac(NEC - 1)
        emit_fb_stt(NEC - 2)
        emit_fb_stt(NEC - 1)
        def p_fb_vdone(vector):
            W(vector, "bs", NEC)
            INC(vector.tensor_copy(aff(F_LN + 0, 1), aff(F_ONC, 1)), "fV", 1)
        on("vector", p_fb_vdone)
        FV_FB = cnt["fV"]

        if debug_stage == "x2pre":
            emit_store_out(FV_FB)
            ps_stack3.close()
            return nc

        emit_ln(FV_FB)
        FV_LN2 = cnt["fV"]
        emit_store_out(FV_LN2, per_tile=list(ln_tile_done))
        ps_stack3.close()

    return nc


def _host_shared(Wq, bq, Wk, bk, Wv, bv, w1, w3, w2):
    f32 = np.float32
    wq_sb = np.ascontiguousarray(
        Wq.reshape(HH, NEC, 128, DD).transpose(0, 2, 1, 3).reshape(HH, 128, 2048),
        dtype=f32)
    wk_sb = np.ascontiguousarray(
        Wk.reshape(GG, NEC, 128, DD).transpose(0, 2, 1, 3).reshape(GG, 128, 2048),
        dtype=f32)
    wv_cat = np.concatenate([Wv[g] for g in range(GG)], axis=1)
    wv_sb = np.ascontiguousarray(
        wv_cat.reshape(NEC, 128, 512).transpose(1, 0, 2).reshape(128, 8192), dtype=f32)

    w1T = np.zeros((EE, HIDP), dtype=f32)
    w1T[:, :HID] = w1.T
    w1_sb = np.ascontiguousarray(
        w1T.reshape(NEC, 128, NHC, 128).transpose(2, 1, 0, 3).reshape(NHC, 128, 2048),
        dtype=f32)
    w3T = np.zeros((EE, HIDP), dtype=f32)
    w3T[:, :HID] = w3.T
    w3_sb = np.ascontiguousarray(
        w3T.reshape(NEC, 128, NHC, 128).transpose(2, 1, 0, 3).reshape(NHC, 128, 2048),
        dtype=f32)
    w2T = np.zeros((HIDP, EE), dtype=f32)
    w2T[:HID, :] = w2.T
    w2_sb = np.ascontiguousarray(
        w2T.reshape(NHC, 128, NEC, 128).transpose(2, 1, 0, 3).reshape(NEC, 128, HIDP),
        dtype=f32)

    constf = np.zeros((128, 432), dtype=f32)
    constf[:, F_IDF:F_IDF + 128] = np.eye(128, dtype=f32)
    constf[:, F_BQT:F_BQT + HH] = bq.T
    constf[:, F_BKT:F_BKT + GG] = bk.T
    constf[0, F_ONR:F_ONR + 128] = 1.0
    constf[:, F_ONC] = 1.0
    constf[:, F_M16] = -16.0
    constf[:, F_EPS] = 1e-5

    bv_cat = np.ascontiguousarray(
        np.concatenate([bv[g] for g in range(GG)]).reshape(1, 512), dtype=f32)
    constb = np.zeros((128, 768), dtype=ml_dtypes.bfloat16)
    constb[:, 0:128] = np.eye(128, dtype=np.float32).astype(ml_dtypes.bfloat16)
    constb[:, 128:256] = np.float32(1.0)

    masks = []
    for v in range(2):
        mk = np.zeros((128, 2048), dtype=ml_dtypes.bfloat16)
        for kc in range(16):
            nv = NV[kc]
            jstar = nv - 1
            rstar = _r_of_slot(v, jstar)
            p = np.arange(128)[:, None]
            q = np.arange(128)[None, :]
            mk[:, kc * 128:(kc + 1) * 128] = (
                (kc * 128 + p) <= (rstar * 128 + q)).astype(ml_dtypes.bfloat16)
            for j in range(nv - 1):
                assert kc * 128 + 127 < _r_of_slot(v, j) * 128, (v, kc, j)
            for j in range(nv, 8):
                assert NK[j] <= kc
        masks.append(mk)

    return dict(wq_sb=wq_sb, wk_sb=wk_sb, wv_sb=wv_sb, w1_sb=w1_sb, w3_sb=w3_sb,
                w2_sb=w2_sb, constf=constf, constb=constb, bvf=bv_cat), masks


def kernel(L0_attn_x, L0_attn_Wq, L0_attn_bq, L0_attn_Wk, L0_attn_bk,
           L0_attn_Wv, L0_attn_bv, L0_ffn_w1, L0_ffn_w3, L0_ffn_w2):
    x = np.asarray(L0_attn_x, dtype=np.float32)
    shared, masks = _host_shared(
        np.asarray(L0_attn_Wq, dtype=np.float32),
        np.asarray(L0_attn_bq, dtype=np.float32),
        np.asarray(L0_attn_Wk, dtype=np.float32),
        np.asarray(L0_attn_bk, dtype=np.float32),
        np.asarray(L0_attn_Wv, dtype=np.float32),
        np.asarray(L0_attn_bv, dtype=np.float32),
        np.asarray(L0_ffn_w1, dtype=np.float32),
        np.asarray(L0_ffn_w3, dtype=np.float32),
        np.asarray(L0_ffn_w2, dtype=np.float32))

    in_maps = []
    rows_per_core = []
    for c in range(8):
        b, v = divmod(c, 2)
        rows = np.concatenate([np.arange(_r_of_slot(v, s) * 128,
                                         (_r_of_slot(v, s) + 1) * 128)
                               for s in range(8)])
        rows_per_core.append((b, rows))
        xb = x[b]
        m = dict(shared)
        m["xkvT"] = np.ascontiguousarray(
            xb.T.reshape(NEC, 128, SS).transpose(1, 0, 2).reshape(128, NEC * SS))
        m["xqT"] = np.ascontiguousarray(
            xb[rows].T.reshape(NEC, 128, 1024).transpose(1, 0, 2).reshape(128, NEC * 1024))
        m["xq"] = np.ascontiguousarray(xb[rows])
        m["masks"] = masks[v]
        in_maps.append(m)

    nc = _build_nc(DEBUG_STAGE)
    res = run_bass_kernel_spmd(nc, in_maps, core_ids=list(range(8)))

    y = np.empty((BB, SS, EE), dtype=np.float32)
    for c in range(8):
        b, rows = rows_per_core[c]
        y[b][rows] = res.results[c]["out"]
    return y


# version 20 (final)
# speedup vs baseline: 1.0645x; 1.0645x over previous
import contextlib
import numpy as np
import ml_dtypes
import concourse.bass as bass
import concourse.mybir as mybir
from concourse.bass_utils import run_bass_kernel_spmd

F32 = mybir.dt.float32
BF16 = mybir.dt.bfloat16
AF = mybir.ActivationFunctionType
ALU = mybir.AluOpType
AX = mybir.AxisListType

BB, SS, EE, HH, GG, DD = 4, 2048, 2048, 16, 4, 128
HID, HIDP, NHC, NEC = 5461, 5504, 43, 16
NK = [16, 14, 12, 10, 8, 6, 4, 2]
NV = [(17 - kc) // 2 for kc in range(16)]
OFF = [0]
for _kc in range(16):
    OFF.append(OFF[-1] + NV[_kc] * 128)
CSM = float(np.float32(1.0 / np.sqrt(128.0)))
CSM2 = float(np.float32((1.0 / np.sqrt(128.0)) / CSM))
WCNT = float(np.float32(0.5 / (HID * EE)))
WCNT2 = float(np.float32((0.5 / (HID * EE)) / WCNT))
INVN = float(np.float32(1.0 / 2048.0))

DEBUG_STAGE = None

A_XKV = 0
A_QFM = 16384
A_OFM = 0
A_KFM = 32768
A_VTM = 40960
A_WK = 49152
A_WV = 57344
A_WQ = 49152
A_PT = 49152
A_MSK = 65536
A_G = 16384
A_W1Q = 60416
A_W3Q = 64512
A_SQD = 49152
A_W2Q = 0
A_W2QS = 58368
A_CBF = 67584
BF16_COLS = 68608

F_IDF = 0
F_BQT = 128
F_BKT = 144
F_ONR = 148
F_ONC = 276
F_M16 = 277
F_EPS = 278
F_UC = 279
F_NUC = 282
F_S3R = 285
F_LN = 288
F_P3 = 296
F_CK = 299
F_NS = 300
F_MF = 16816
F_RC = 304
F_X1 = 432
F_ABS = 432
F_RSB = 12720
F_PAR = 14768
F_QIN = 4528
F_AB2 = 432
F_W1S = 432
F_W3S = 4528
F_SIL = 8624
F_W2S = 16816
F_FSC = 18192
F32_COLS = 18704


def _r_of_slot(v, s):
    k = 7 - s
    return 2 * k + (k & 1) if v == 0 else 2 * k + 1 - (k & 1)


def _build_nc(debug_stage=None):
    nc = bass.Bass("TRN2", target_bir_lowering=False)

    d_xkvT = nc.dram_tensor("xkvT", [128, 32768], F32, kind="ExternalInput").ap()
    d_xqT = nc.dram_tensor("xqT", [128, 16384], F32, kind="ExternalInput").ap()
    d_xq = nc.dram_tensor("xq", [1024, EE], F32, kind="ExternalInput").ap()
    d_wq = nc.dram_tensor("wq_sb", [HH, 128, 2048], F32, kind="ExternalInput").ap()
    d_wk = nc.dram_tensor("wk_sb", [GG, 128, 2048], F32, kind="ExternalInput").ap()
    d_wv = nc.dram_tensor("wv_sb", [128, 8192], F32, kind="ExternalInput").ap()
    d_w1 = nc.dram_tensor("w1_sb", [NHC, 128, 2048], F32, kind="ExternalInput").ap()
    d_w3 = nc.dram_tensor("w3_sb", [NHC, 128, 2048], F32, kind="ExternalInput").ap()
    d_w2 = nc.dram_tensor("w2_sb", [NEC, 128, HIDP], F32, kind="ExternalInput").ap()
    d_cf = nc.dram_tensor("constf", [128, 432], F32, kind="ExternalInput").ap()
    d_bv = nc.dram_tensor("bvf", [1, 512], F32, kind="ExternalInput").ap()
    d_cb = nc.dram_tensor("constb", [128, 768], BF16, kind="ExternalInput").ap()
    d_msk = nc.dram_tensor("masks", [128, 2048], BF16, kind="ExternalInput").ap()
    d_out = nc.dram_tensor("out", [1024, EE], F32, kind="ExternalOutput").ap()
    d_sp = nc.dram_tensor("x1spill", [1024, EE], F32).ap()
    d_w2q = nc.dram_tensor("w2qs", [NEC, 128, HIDP], BF16).ap()

    cnt = {}

    with contextlib.ExitStack() as ctx:
        ab = ctx.enter_context(nc.sbuf_tensor("ab", [128, BF16_COLS], BF16))
        af = ctx.enter_context(nc.sbuf_tensor("af", [128, F32_COLS], F32))

        sems = {}
        for name in ["ld", "lg", "la", "lb", "st", "lc", "ls",
                     "lva", "lvb", "lvc", "lvd", "lwk", "lwv",
                     "lx0", "lx1", "lx2", "lx3", "lx4", "lx5", "lx6", "lx7",
                     "lq5", "lqa", "lqb", "lqc", "lqd",
                     "fo", "fx", "fT", "fS", "fV",
                     "sc", "ex", "mk", "av", "rev", "rc", "oev",
                     "pm", "pe", "fq", "fm", "fg", "sil",
                     "bq", "bs", "bm", "be", "bt"]:
            cnt[name] = 0
            sems[name] = ctx.enter_context(nc.semaphore(name))

        def INC(instr, name, k=1):
            instr.then_inc(sems[name], k)
            cnt[name] += k
            return cnt[name]

        def W(eng, name, val):
            if val > 0:
                eng.wait_ge(sems[name], val)

        blk = ctx.enter_context(nc.Block())

        def on(engname, fn):
            getattr(blk, engname)(fn)

        def abf(c0, n):
            return ab[:, c0:c0 + n]

        def aff(c0, n):
            return af[:, c0:c0 + n]

        identf = aff(F_IDF, 128)
        identb = abf(A_CBF, 128)
        ones128 = abf(A_CBF + 128, 128)

        onesrow = af[0:1, F_ONR:F_ONR + 128]
        onescol = aff(F_ONC, 1)
        m16col = aff(F_M16, 1)
        epscol = aff(F_EPS, 1)

        def p0_sync(sync):
            INC(sync.dma_start(out=aff(0, 432), in_=d_cf), "ld", 16)
            INC(sync.dma_start(out=ab[:, A_CBF:A_CBF + 768], in_=d_cb), "ld", 16)
            INC(sync.dma_start(out=abf(A_MSK, 2048), in_=d_msk), "ld", 16)
            INC(sync.dma_start(out=af[0:1, F_FSC:F_FSC + 512], in_=d_bv), "ld", 16)
        on("sync", p0_sync)
        LD_CONSTS = cnt["ld"]

        def ckpt(sync):
            sync.wait_ge(sems["ld"], cnt["ld"])
            INC(sync.dma_start(out=af[0:1, F_CK:F_CK + 1], in_=d_cf[0:1, 0:1]),
                "lc", 16)

        def p0_ck(sync):
            ckpt(sync)
        on("sync", p0_ck)
        LC_CONSTS = cnt["lc"]

        def p0_gp(gpsimd):
            wk_ap = bass.AP(d_wk.tensor, 0, [[2048, 128], [128 * 2048, GG], [1, 2048]])
            INC(gpsimd.dma_start(out=abf(A_WK, 8192), in_=wk_ap), "lwk", 16)
            INC(gpsimd.dma_start(out=abf(A_WV, 8192), in_=d_wv), "lwv", 16)
            for j in range(8):
                INC(gpsimd.dma_start(out=abf(A_XKV + j * 4096, 4096),
                                     in_=d_xkvT[:, j * 4096:(j + 1) * 4096]),
                    "lx%d" % j, 16)
        on("gpsimd", p0_gp)
        LG_KV0 = cnt["lg"]

        ps_stack = contextlib.ExitStack()
        psA = ps_stack.enter_context(nc.psum_tensor("psA", [128, 1024], F32))
        psB = ps_stack.enter_context(nc.psum_tensor("psB", [128, 1024], F32))
        psR = ps_stack.enter_context(nc.psum_tensor("psR", [128, 1024], F32))
        psO = ps_stack.enter_context(nc.psum_tensor("psO", [128, 1024], F32))
        psQK = [psA, psB, psR, psO]

        unit = [0]

        def proj_unit(mm_fn, ev_fn):
            u = unit[0]
            unit[0] += 1

            def tfun(tensor, u=u, mm_fn=mm_fn):
                W(tensor, "pe", max(0, u - 3))
                last = mm_fn(tensor)
                INC(last, "pm", 1)
            on("tensor", tfun)

            def vfun(vector, u=u, ev_fn=ev_fn):
                if u == 0:
                    W(vector, "lc", LC_CONSTS)
                W(vector, "pm", u + 1)
                last = ev_fn(vector)
                INC(last, "pe", 1)
            on("vector", vfun)

        for g in range(GG):
            for half in range(2):
                ps = psQK[unit[0] % 4]

                def mk_mm(g=g, half=half, ps=ps, u0=unit[0]):
                    def mm(tensor):
                        if u0 < 2:
                            W(tensor, "lwk", 16)
                        elif u0 == 2:
                            for s in ["lwk", "lwv", "lx0", "lx1", "lx2", "lx3",
                                      "lx4", "lx5", "lx6", "lx7"]:
                                W(tensor, s, 16)
                        last = None
                        for ec in range(NEC):
                            if u0 < 2:
                                W(tensor, "lx%d" % (ec // 2), 16)
                            lhs = abf(A_WK + g * 2048 + ec * 128, 128)
                            for b2 in range(2):
                                last = tensor.matmul(
                                    ps[:, b2 * 512:(b2 + 1) * 512], lhs,
                                    abf(A_XKV + ec * 2048 + half * 1024 + b2 * 512,
                                        512),
                                    start=(ec == 0), stop=(ec == NEC - 1))
                        return last
                    return mm

                def mk_ev(g=g, ps=ps, half=half):
                    def ev(vector):
                        return vector.tensor_scalar(
                            abf(A_KFM + g * 2048 + half * 1024, 1024), ps[:],
                            aff(F_BKT + g, 1), CSM, ALU.add, ALU.mult)
                    return ev
                proj_unit(mk_mm(), mk_ev())

        for tt in range(16):
            ps = psQK[unit[0] % 4]

            def mk_mm(tt=tt, ps=ps, first=(tt == 0)):
                def mm(tensor):
                    if first:
                        W(tensor, "ld", LD_CONSTS)
                    tensor.matmul(ps[:, 0:512], onesrow,
                                  af[0:1, F_FSC:F_FSC + 512],
                                  start=True, stop=False)
                    last = None
                    for ec in range(NEC):
                        lhs = abf(A_XKV + ec * 2048 + tt * 128, 128)
                        last = tensor.matmul(ps[:, 0:512], lhs,
                                             abf(A_WV + ec * 512, 512),
                                             start=False, stop=(ec == NEC - 1))
                    return last
                return mm

            def mk_ev(tt=tt, ps=ps):
                def ev(vector):
                    return vector.tensor_copy(abf(A_VTM + tt * 512, 512),
                                              ps[:, 0:512])
                return ev
            proj_unit(mk_mm(), mk_ev())

        LG_XQ = LG_KV0 + 4 * 16

        def p_gp2(gpsimd):
            gpsimd.wait_ge(sems["pm"], 24)
            for j in range(4):
                INC(gpsimd.dma_start(out=abf(A_XKV + j * 4096, 4096),
                                     in_=d_xqT[:, j * 4096:(j + 1) * 4096]),
                    "lg", 16)
            for h in range(HH):
                if h >= 2:
                    gpsimd.wait_ge(sems["pm"], 24 + (h - 1))
                INC(gpsimd.dma_start(out=abf(A_WQ + (h % 2) * 2048, 2048),
                                     in_=d_wq[h]), "lg", 16)
        on("gpsimd", p_gp2)

        _mini = {}

        def emit_mini_finalize():
            def p_mini_v1(vector):
                W(vector, "fS", NEC)
                INC(vector.reduce_sum(aff(F_MF + 3, 1), aff(F_PAR + 0, NEC), AX.X),
                    "fq", 1)
            on("vector", p_mini_v1)

            def p_mini_t1(tensor):
                W(tensor, "pe", 31)
                W(tensor, "fq", 1)
                INC(tensor.matmul(psR[0:1, 0:1], onescol, aff(F_MF + 3, 1),
                                  start=True, stop=True), "fT", 1)
            on("tensor", p_mini_t1)
            FT_MF1 = cnt["fT"]

            def p_mini_v2(vector):
                W(vector, "fT", FT_MF1)
                INC(vector.tensor_scalar(af[0:1, F_MF:F_MF + 1], psR[0:1, 0:1],
                                         WCNT, WCNT2, ALU.mult, ALU.mult), "fq", 1)
            on("vector", p_mini_v2)

            def p_mini_t2(tensor):
                W(tensor, "fq", 2)
                INC(tensor.matmul(psR[:, 4:5], onesrow[:, 0:128],
                                  af[0:1, F_MF:F_MF + 1], start=True, stop=True),
                    "fT", 1)
            on("tensor", p_mini_t2)
            FT_MF2 = cnt["fT"]
            _mini["FT_MF2"] = FT_MF2

            def p_mini_v3(vector):
                W(vector, "fT", FT_MF2)
                vector.tensor_copy(aff(F_MF + 1, 1), psR[:, 4:5])
                INC(vector.tensor_scalar(aff(F_MF + 2, 1), psR[:, 4:5], -1.0, None,
                                         ALU.mult), "fq", 1)
            on("vector", p_mini_v3)
            pass
            return cnt["fq"]

        FQ_MF = 3
        for h in range(HH):
            if h == 7:
                emit_mini_finalize()
            ps = psQK[unit[0] % 4]
            lg_wq_h = LG_XQ + (h + 1) * 16

            def mk_mm(h=h, ps=ps, lg_wq_h=lg_wq_h):
                def mm(tensor):
                    W(tensor, "lg", LG_XQ + 16 * min(h + 2, HH))
                    if h == 10:
                        W(tensor, "fq", FQ_MF)
                    last = None
                    for ec in range(NEC):
                        lhs = abf(A_WQ + (h % 2) * 2048 + ec * 128, 128)
                        for b2 in range(2):
                            last = tensor.matmul(
                                ps[:, b2 * 512:(b2 + 1) * 512], lhs,
                                abf(A_XKV + ec * 1024 + b2 * 512, 512),
                                start=(ec == 0), stop=(ec == NEC - 1))
                    return last
                return mm

            def mk_ev(h=h, ps=ps):
                def ev(vector):
                    return vector.tensor_scalar(
                        abf(A_QFM + h * 1024, 1024), ps[:],
                        aff(F_BQT + h, 1), None, ALU.add)
                return ev
            proj_unit(mk_mm(), mk_ev())

        NUNITS = unit[0]

        ab2_pieces = [("w1", hc) for hc in range(NHC)] +                      [("w3", hc) for hc in range(NHC)]
        NAB2 = len(ab2_pieces)

        ph1_ld = {}

        xq_ld = {}

        def p_abs_sync(sync):
            W(sync, "ld", LD_CONSTS)
            for i in range(1, NEC, 2):
                if i >= 3:
                    W(sync, "fS", i - 1)
                ph1_ld[i] = INC(sync.dma_start(out=aff(F_ABS + 6144, HIDP),
                                               in_=d_w2[i]), "ld", 16)
            W(sync, "fS", NEC)
            INC(sync.dma_start(out=aff(F_X1 + 5 * 2048, 2048),
                               in_=d_xq[5 * 128:6 * 128, :]), "lq5", 16)
        on("sync", p_abs_sync)

        def p_abs_scalar(scalar):
            INC(scalar.dma_start(out=aff(F_ABS, HIDP), in_=d_w2[0]), "ls", 16)
            for i in range(NEC):
                if i % 2 == 0:
                    W(scalar, "ls", 16 * (i // 2 + 1))
                else:
                    W(scalar, "ld", ph1_ld[i])
                buf = aff(F_ABS + (i % 2) * 6144, HIDP)
                INC(scalar.activation(buf, buf, AF.Abs,
                                      accum_out=aff(F_PAR + i, 1)), "fS", 1)
                if i % 2 == 0 and i + 2 < NEC:
                    W(scalar, "fS", i + 1)
                    INC(scalar.dma_start(out=aff(F_ABS, HIDP),
                                         in_=d_w2[i + 2]), "ls", 16)
        on("scalar", p_abs_scalar)
        FS_PRE = cnt["fS"]


        def p_ab2_gp(gpsimd):
            W(gpsimd, "fS", FS_PRE)
            for j in range(NAB2):
                w, hc = ab2_pieces[j]
                dsrc = d_w1 if w == "w1" else d_w3
                if j >= 2:
                    W(gpsimd, "fS", FS_PRE + j - 1)
                INC(gpsimd.dma_start(out=aff(F_AB2 + (j % 2) * 2048, 2048),
                                     in_=dsrc[hc]), "la" if j % 2 == 0 else "lb", 16)
        on("gpsimd", p_ab2_gp)

        def emit_ab2_scalar(j):
            def sfun(scalar, j=j):
                W(scalar, "la" if j % 2 == 0 else "lb", 16 * (j // 2 + 1))
                buf = aff(F_AB2 + (j % 2) * 2048, 2048)
                INC(scalar.activation(buf, buf, AF.Abs,
                                      accum_out=aff(F_PAR + NEC + j, 1)), "fS", 1)
            on("scalar", sfun)



        w2p_ld = {}

        HQ = HIDP // 4

        def p_w2p_sync(sync):
            for ec in range(1, NEC + 1):
                W(sync, "fq", FQ_MF + 2 * (4 * ec))
                w2p_ld[("out", ec - 1)] = INC(
                    sync.dma_start(out=d_w2q[ec - 1], in_=abf(A_W2QS, HIDP)),
                    "ld", 16)
        on("sync", p_w2p_sync)

        NQUAR = 4 * NEC
        QSEM = ["lva", "lvb", "lvc", "lvd"]

        def emit_qin_push(q):
            def sfun(scalar, q=q):
                ec, quar = divmod(q, 4)
                if q < 4:
                    W(scalar, "fS", NEC)
                else:
                    W(scalar, "fq", FQ_MF + 2 * (q - 3))
                INC(scalar.dma_start(out=aff(F_QIN + (q % 4) * HQ, HQ),
                                     in_=d_w2[ec][:, quar * HQ:(quar + 1) * HQ]),
                    QSEM[q % 4], 16)
            on("scalar", sfun)

        def emit_w2p_quarter(q):
            ec, quar = divmod(q, 4)

            def vfun(vector, q=q, ec=ec, quar=quar):
                if q == 0:
                    W(vector, "fq", FQ_MF)
                    W(vector, "pm", 24)
                if quar == 0 and ec >= 1:
                    W(vector, "ld", w2p_ld[("out", ec - 1)])
                W(vector, QSEM[q % 4], 16 * (q // 4 + 1))
                st = aff(F_QIN + (q % 4) * HQ, HQ)
                qq = abf(A_W2QS + quar * HQ, HQ)
                n1 = INC(vector.tensor_scalar(qq, st, aff(F_MF + 1, 1), None,
                                              ALU.is_gt), "fq", 1)
                W(vector, "fq", n1)
                INC(vector.scalar_tensor_tensor(qq, st, aff(F_MF + 2, 1), qq,
                                                ALU.is_lt, ALU.subtract), "fq", 1)
            on("vector", vfun)

        ALAG = 3

        def p_att_tensor(tensor):
            W(tensor, "pe", NUNITS)
            W(tensor, "lc", LC_CONSTS)
            for idx in range(256 + ALAG):
                if idx < 256:
                    h, kc = divmod(idx, 16)
                    N = NV[kc] * 128
                    g = h // 4
                    ps = psA if kc % 2 == 0 else psB
                    W(tensor, "ex", idx - 1)
                    last = None
                    for b2 in range((N + 511) // 512):
                        n2 = min(512, N - b2 * 512)
                        last = tensor.matmul(
                            ps[:, b2 * 512:b2 * 512 + n2],
                            abf(A_KFM + g * 2048 + kc * 128, 128),
                            abf(A_QFM + h * 1024 + b2 * 512, n2),
                            start=True, stop=True)
                    INC(last, "sc", 1)
                if idx >= ALAG:
                    jdx = idx - ALAG
                    h, kc = divmod(jdx, 16)
                    N = NV[kc] * 128
                    g = h // 4
                    W(tensor, "mk", jdx + 1)
                    if jdx == 0:
                        W(tensor, "fq", FQ_MF)
                    if kc == 0:
                        W(tensor, "rev", h)
                        W(tensor, "rc", h)
                        W(tensor, "oev", h)
                    for b2 in range((N + 511) // 512):
                        n2 = min(512, N - b2 * 512)
                        stop = (kc == 15) if b2 == 0 else (kc == 7)
                        tensor.matmul(psR[:, b2 * 512:b2 * 512 + n2], ones128,
                                      abf(A_PT + OFF[kc] + b2 * 512, n2),
                                      start=(kc == 0), stop=stop)
                    last = None
                    for b2 in range((N + 511) // 512):
                        n2 = min(512, N - b2 * 512)
                        stop = (kc == 15) if b2 == 0 else (kc == 7)
                        last = tensor.matmul(
                            psO[:, b2 * 512:b2 * 512 + n2],
                            abf(A_VTM + kc * 512 + g * 128, 128),
                            abf(A_PT + OFF[kc] + b2 * 512, n2),
                            start=(kc == 0), stop=stop)
                    INC(last, "av", 1)
                    if kc == 15:
                        W(tensor, "rev", h + 1)
                        last = None
                        for j in range(8):
                            last = tensor.transpose(
                                psR[:, j * 128:(j + 1) * 128],
                                aff(F_RSB + (h % 2) * 1024 + j * 128, 128), identf)
                        INC(last, "fT", 1)
        on("tensor", p_att_tensor)
        FT_ATT = cnt["fT"]

        def p_att_scalar_idx(idx):
            def sfun(scalar, idx=idx):
                if idx < 256:
                    h, kc = divmod(idx, 16)
                    N = NV[kc] * 128
                    ps = psA if kc % 2 == 0 else psB
                    W(scalar, "sc", idx + 1)
                    if idx >= 16:
                        W(scalar, "av", idx - 15)
                    INC(scalar.activation(abf(A_PT + OFF[kc], N), ps[:, 0:N],
                                          AF.Exp, scale=CSM2, bias=m16col),
                        "ex", 1)
                tidx = idx - ALAG
                if tidx >= 0 and tidx % 16 == 15:
                    h2 = tidx // 16
                    W(scalar, "av", tidx + 1)
                    INC(scalar.activation(abf(A_OFM + h2 * 1024, 1024), psO[:],
                                          AF.Copy), "oev", 1)
            on("scalar", sfun)

        _nab = [0]
        for q0 in range(4):
            emit_qin_push(q0)
        for idx in range(256 + ALAG):
            p_att_scalar_idx(idx)
            if idx % 4 == 0 and idx >= 8 and idx // 4 + 2 < NQUAR:
                emit_qin_push(idx // 4 + 2)
            want = min(((idx + 1) * NAB2) // 256, NAB2)
            while _nab[0] < want:
                emit_ab2_scalar(_nab[0])
                _nab[0] += 1
        while _nab[0] < NAB2:
            emit_ab2_scalar(_nab[0])
            _nab[0] += 1
        FS_ABS = FS_PRE + NAB2

        def p_att_vector_idx(idx):
            def vfun(vector, idx=idx):
                if idx == 0:
                    W(vector, "lc", LC_CONSTS)
                if idx < 256:
                    h, kc = divmod(idx, 16)
                    nv = NV[kc]
                    W(vector, "ex", idx + 1)
                    pslab = abf(A_PT + OFF[kc] + (nv - 1) * 128, 128)
                    INC(vector.tensor_tensor(pslab, pslab,
                                             abf(A_MSK + kc * 128, 128), ALU.mult),
                        "mk", 1)
                tidx = idx - ALAG
                if tidx >= 0 and tidx % 16 == 15:
                    h2 = tidx // 16
                    W(vector, "av", tidx + 1)
                    INC(vector.tensor_copy(aff(F_RSB + (h2 % 2) * 1024, 1024),
                                           psR[:]), "rev", 1)
                    W(vector, "fT", _mini["FT_MF2"] + h2 + 1)
                    last = None
                    for j in range(8):
                        last = vector.reciprocal(aff(F_RC + h2 * 8 + j, 1),
                                                 psR[:, j * 128:j * 128 + 1])
                    INC(last, "rc", 1)
            on("vector", vfun)

        for idx in range(256 + ALAG):
            p_att_vector_idx(idx)
            if idx % 4 == 2 and idx // 4 < NQUAR:
                emit_w2p_quarter(idx // 4)
        FQ_W2END = cnt["fq"]

        def p_fin_vector(vector):
            W(vector, "fS", FS_ABS)
            vector.reduce_sum(aff(F_P3 + 0, 1), aff(F_PAR + NEC, NHC), AX.X)
            vector.reduce_sum(aff(F_P3 + 1, 1), aff(F_PAR + NEC + NHC, NHC), AX.X)
            INC(vector.reduce_sum(aff(F_P3 + 2, 1), aff(F_PAR + 0, NEC), AX.X),
                "fV", 1)
        on("vector", p_fin_vector)
        FV_P3 = cnt["fV"]

        def p_fin_tensor(tensor):
            W(tensor, "fV", FV_P3)
            W(tensor, "ex", 256)
            INC(tensor.matmul(psB[0:1, 0:3], onescol, aff(F_P3, 3),
                              start=True, stop=True), "fT", 1)
        on("tensor", p_fin_tensor)
        FT_S3 = cnt["fT"]

        def p_fin_vector2(vector):
            W(vector, "fT", FT_S3)
            INC(vector.tensor_scalar(af[0:1, F_S3R:F_S3R + 3], psB[0:1, 0:3],
                                     WCNT, WCNT2, ALU.mult, ALU.mult), "fV", 1)
        on("vector", p_fin_vector2)
        FV_S3 = cnt["fV"]

        def p_fin_tensor2(tensor):
            W(tensor, "fV", FV_S3)
            INC(tensor.matmul(psB[:, 4:7], onesrow, af[0:1, F_S3R:F_S3R + 3],
                              start=True, stop=True), "fT", 1)
        on("tensor", p_fin_tensor2)
        FT_REP = cnt["fT"]

        def p_fin_vector3(vector):
            W(vector, "fT", FT_REP)
            vector.tensor_copy(aff(F_UC, 3), psB[:, 4:7])
            vector.tensor_scalar(aff(F_NUC, 3), psB[:, 4:7], -1.0, None, ALU.mult)
            INC(vector.tensor_scalar(aff(F_NS, 3), psB[:, 4:7], -2.0, None,
                                     ALU.mult), "fV", 1)
        on("vector", p_fin_vector3)
        FV_U = cnt["fV"]

        JORD = [5, 2, 3, 4, 0, 1, 6, 7]

        XQW = {5: ("lq5", 16), 2: ("lqa", 48), 3: ("lqa", 48), 4: ("lqa", 48),
               0: ("lqb", 32), 1: ("lqb", 32), 6: ("lqc", 16), 7: ("lqd", 16)}

        def p_attb_sync(sync):
            W(sync, "fq", FQ_W2END)
            for t in (2, 3, 4):
                INC(sync.dma_start(out=aff(F_X1 + t * 2048, 2048),
                                   in_=d_xq[t * 128:(t + 1) * 128, :]), "lqa", 16)
            W(sync, "fS", FS_ABS)
            for t in (0, 1):
                INC(sync.dma_start(out=aff(F_X1 + t * 2048, 2048),
                                   in_=d_xq[t * 128:(t + 1) * 128, :]), "lqb", 16)
            W(sync, "fT", FT_ATT)
            INC(sync.dma_start(out=aff(F_X1 + 6 * 2048, 2048),
                               in_=d_xq[6 * 128:7 * 128, :]), "lqc", 16)
            W(sync, "fV", FV_U)
            INC(sync.dma_start(out=aff(F_X1 + 7 * 2048, 2048),
                               in_=d_xq[7 * 128:8 * 128, :]), "lqd", 16)
        on("sync", p_attb_sync)

        ps_stack.close()
        ps_stack2 = contextlib.ExitStack()
        pOT = [ps_stack2.enter_context(nc.psum_tensor("pOT%d" % i, [128, 128], BF16))
               for i in range(2)]

        _pi = [0]
        for jpos, j in enumerate(JORD):
            def tfun(tensor, j=j, first=(jpos == 0)):
                if first:
                    W(tensor, "oev", 16)
                    W(tensor, "rc", 16)
                for h in range(HH):
                    pi = _pi[0]
                    if pi >= 2:
                        W(tensor, "fo", pi - 1)
                    INC(tensor.transpose(pOT[pi % 2][:],
                                         abf(A_OFM + h * 1024 + j * 128, 128),
                                         identb), "fT", 1)
                    _pi[0] += 1
            on("tensor", tfun)

            def vfun(vector, j=j, pi0=jpos * HH):
                W(vector, XQW[j][0], XQW[j][1])
                for h in range(HH):
                    pi = pi0 + h
                    W(vector, "fT", FT_REP + pi + 1)
                    m = h * 8 + j
                    x1s = aff(F_X1 + j * 2048 + h * 128, 128)
                    INC(vector.scalar_tensor_tensor(
                        x1s, pOT[pi % 2][:], aff(F_RC + m, 1), x1s,
                        ALU.mult, ALU.add), "fo", 1)
            on("vector", vfun)
        FT_OB = cnt["fT"]

        def emit_store_out(wait_v, per_tile=None):
            def p_st(sync):
                for t in range(8):
                    W(sync, "fV", per_tile[t] if per_tile else wait_v)
                    INC(sync.dma_start(out=d_out[t * 128:(t + 1) * 128, :],
                                       in_=aff(F_X1 + t * 2048, 2048)), "st", 16)
                sync.wait_ge(sems["st"], cnt["st"])
            on("sync", p_st)

        ln_tile_done = []

        def emit_ln(first_wait_v, tiles=None, extra=None):
            del ln_tile_done[:]
            ln_done = {}
            prev_v = [first_wait_v]
            for tpos, t in enumerate(tiles if tiles is not None else range(8)):
                ex_w = extra(tpos) if extra else []

                def sfun(scalar, t=t, pv=prev_v[0], ex_w=ex_w):
                    for s, v in ex_w:
                        W(scalar, s, v)
                    W(scalar, "fV", pv)
                    INC(scalar.activation(abf(A_SQD, 2048), aff(F_X1 + t * 2048, 2048),
                                          AF.Square, accum_out=aff(F_LN + 1, 1)),
                        "fS", 1)
                on("scalar", sfun)

                def vfun(vector, t=t, pv=prev_v[0], ex_w=ex_w):
                    for s, v in ex_w:
                        W(vector, s, v)
                    W(vector, "fV", pv)
                    INC(vector.reduce_sum(aff(F_LN + 0, 1), aff(F_X1 + t * 2048, 2048),
                                          AX.X), "fV", 1)
                on("vector", vfun)
                v_sum = cnt["fV"]

                def sfun2(scalar, v_sum=v_sum):
                    W(scalar, "fV", v_sum)
                    scalar.mul(aff(F_LN + 2, 1), aff(F_LN + 0, 1), INVN)
                    INC(scalar.activation(aff(F_LN + 3, 1), aff(F_LN + 0, 1),
                                          AF.Square, scale=INVN), "fS", 1)
                on("scalar", sfun2)
                s_m = cnt["fS"]

                def vfun2(vector, s_m=s_m):
                    W(vector, "fS", s_m)
                    INC(vector.scalar_tensor_tensor(aff(F_LN + 4, 1), aff(F_LN + 1, 1),
                                                    INVN, aff(F_LN + 3, 1),
                                                    ALU.mult, ALU.subtract), "fV", 1)
                on("vector", vfun2)
                v_var = cnt["fV"]

                def sfun3(scalar, v_var=v_var):
                    W(scalar, "fV", v_var)
                    INC(scalar.activation(aff(F_LN + 5, 1), aff(F_LN + 4, 1),
                                          AF.Sqrt, bias=epscol), "fS", 1)
                on("scalar", sfun3)
                s_std = cnt["fS"]

                def vfun3(vector, t=t, s_std=s_std):
                    W(vector, "fS", s_std)
                    rc_done = INC(vector.reciprocal(aff(F_LN + 6, 1),
                                                    aff(F_LN + 5, 1)), "fV", 1)
                    W(vector, "fV", rc_done)
                    INC(vector.tensor_scalar(aff(F_X1 + t * 2048, 2048),
                                             aff(F_X1 + t * 2048, 2048),
                                             aff(F_LN + 2, 1), aff(F_LN + 6, 1),
                                             ALU.subtract, ALU.mult), "fV", 1)
                on("vector", vfun3)
                prev_v[0] = cnt["fV"]
                ln_tile_done.append(cnt["fV"])
                ln_done[t] = cnt["fV"]
            return ln_done

        ln1_done = emit_ln(cnt["fV"], tiles=JORD,
                           extra=lambda tpos: [("fo", 16 * (tpos + 1))])
        FV_LN1 = cnt["fV"]

        if debug_stage == "x1":
            emit_store_out(FV_LN1)
            ps_stack2.close()
            return nc

        def p_spill_sync(sync):
            for t in JORD:
                W(sync, "fV", ln1_done[t])
                INC(sync.dma_start(out=d_sp[t * 128:(t + 1) * 128, :],
                                   in_=aff(F_X1 + t * 2048, 2048)), "ld", 16)
        on("sync", p_spill_sync)
        LD_SPILL = cnt["ld"]

        ps_stack2.close()
        ps_stack3 = contextlib.ExitStack()
        pH = [ps_stack3.enter_context(nc.psum_tensor("pH%d" % i, [128, 1024], F32))
              for i in range(4)]

        def p_xf_tensor(tensor):
            W(tensor, "fo", 128)
            ti = 0
            for t in JORD:
                W(tensor, "fV", ln1_done[t])
                for ec in range(NEC):
                    if ti >= 8:
                        W(tensor, "fx", ti - 7)
                    slot = pH[ti % 4][:, ((ti // 4) % 2) * 512:((ti // 4) % 2) * 512 + 128]
                    INC(tensor.transpose(slot, aff(F_X1 + t * 2048 + ec * 128, 128),
                                         identf), "fT", 1)
                    ti += 1
        on("tensor", p_xf_tensor)
        FT_XF = cnt["fT"]

        def p_xf_vector(vector):
            ti = 0
            for t in JORD:
                for ec in range(NEC):
                    W(vector, "fT", FT_OB + ti + 1)
                    slot = pH[ti % 4][:, ((ti // 4) % 2) * 512:((ti // 4) % 2) * 512 + 128]
                    INC(vector.tensor_copy(abf(A_XKV + ec * 1024 + t * 128, 128), slot),
                        "fx", 1)
                    ti += 1
        on("vector", p_xf_vector)
        FX_XF = cnt["fx"]

        def p_fa_sync(sync):
            W(sync, "ld", LD_SPILL)
            W(sync, "fT", FT_XF)
            for hc in range(NHC):
                for w, base, dsrc in (("w1", F_W1S, d_w1), ("w3", F_W3S, d_w3)):
                    if hc >= 2:
                        W(sync, "bq", (hc - 2) * 4 + (2 if w == "w1" else 4))
                    INC(sync.dma_start(out=aff(base + (hc % 2) * 2048, 2048),
                                       in_=dsrc[hc]), "ld", 16)
        on("sync", p_fa_sync)

        def ld_fa(hc, w):
            ss = hc * 2 + (0 if w == "w1" else 1)
            return LD_SPILL + 16 * min(ss + 4, NHC * 2)

        def emit_fa_quant(hc):
            def vfun(vector, hc=hc):
                if hc == 0:
                    W(vector, "fV", FV_U)
                    W(vector, "fx", FX_XF)
                for w, sbase, qbase, uc in (("w1", F_W1S, A_W1Q, 0),
                                            ("w3", F_W3S, A_W3Q, 1)):
                    W(vector, "ld", ld_fa(hc, w))
                    if hc >= 2:
                        W(vector, "fm", (hc - 2) * 2 + 2)
                    st = aff(sbase + (hc % 2) * 2048, 2048)
                    qq = abf(qbase + (hc % 2) * 2048, 2048)
                    n1 = INC(vector.tensor_scalar(qq, st, aff(F_UC + uc, 1), None,
                                                  ALU.is_gt), "bq", 1)
                    W(vector, "bq", n1)
                    INC(vector.scalar_tensor_tensor(qq, st, aff(F_NUC + uc, 1), qq,
                                                    ALU.is_lt, ALU.subtract), "bq", 1)
            on("vector", vfun)

        def emit_fa_g(hc):
            def vfun(vector, hc=hc):
                par = hc % 2
                W(vector, "fm", hc * 2 + 2)
                W(vector, "sil", hc + 1)
                INC(vector.scalar_tensor_tensor(
                    abf(A_G + hc * 1024, 1024), pH[2 + par][:],
                    aff(F_NS + 1, 1), aff(F_SIL, 1024),
                    ALU.mult, ALU.mult), "fg", 1)
            on("vector", vfun)

        for hc in range(NHC):
            emit_fa_quant(hc)
            if hc >= 2:
                emit_fa_g(hc - 2)
        BQ_FA = cnt["bq"]

        def p_fa_tensor(tensor):
            W(tensor, "fx", FX_XF)
            for hc in range(NHC):
                par = hc % 2
                for w, qbase, psi in (("w1", A_W1Q, 0), ("w3", A_W3Q, 2)):
                    W(tensor, "bq", hc * 4 + (2 if w == "w1" else 4))
                    if hc >= 2:
                        if w == "w1":
                            W(tensor, "sil", hc - 1)
                        else:
                            W(tensor, "fg", hc - 1)
                    ps = pH[psi + par]
                    last = None
                    for ec in range(NEC):
                        lhs = abf(qbase + par * 2048 + ec * 128, 128)
                        for b2 in range(2):
                            last = tensor.matmul(
                                ps[:, b2 * 512:(b2 + 1) * 512], lhs,
                                abf(A_XKV + ec * 1024 + b2 * 512, 512),
                                start=(ec == 0), stop=(ec == NEC - 1))
                    INC(last, "fm", 1)
        on("tensor", p_fa_tensor)
        FM_FA = cnt["fm"]

        def p_fa_scalar(scalar):
            for hc in range(NHC):
                par = hc % 2
                W(scalar, "fm", hc * 2 + 1)
                if hc >= 1:
                    W(scalar, "fg", hc)
                INC(scalar.activation(aff(F_SIL, 1024), pH[0 + par][:], AF.Silu,
                                      scale=aff(F_NS + 0, 1)), "sil", 1)
        on("scalar", p_fa_scalar)

        emit_fa_g(NHC - 2)
        emit_fa_g(NHC - 1)
        FG_ALL = cnt["fg"]

        if debug_stage == "ffna":
            def p_dbg_sync(sync):
                W(sync, "bq", BQ_FA)
                W(sync, "sil", NHC)
                W(sync, "fg", NHC)
                for t in range(8):
                    INC(sync.dma_start(out=aff(F_X1 + t * 2048, 2048),
                                       in_=d_sp[t * 128:(t + 1) * 128, :]), "ld", 16)
                ckpt(sync)
                for t in range(8):
                    INC(sync.dma_start(out=d_out[t * 128:(t + 1) * 128, :],
                                       in_=aff(F_X1 + t * 2048, 2048)), "st", 16)
                sync.wait_ge(sems["st"], cnt["st"])
            on("sync", p_dbg_sync)
            ps_stack3.close()
            return nc

        def p_fb_sync0(sync):
            W(sync, "bq", BQ_FA)
            W(sync, "sil", NHC)
            W(sync, "fg", NHC)
            for t in range(8):
                INC(sync.dma_start(out=aff(F_X1 + t * 2048, 2048),
                                   in_=d_sp[t * 128:(t + 1) * 128, :]), "ld", 16)
            ckpt(sync)
        on("sync", p_fb_sync0)
        LD_X1R = cnt["ld"]
        LC_X1R = cnt["lc"]

        w2l_ld = {}

        def p_fb_sync(sync):
            W(sync, "fm", FM_FA)
            for ec in range(NEC):
                if ec >= 2:
                    W(sync, "bm", ec - 1)
                w2l_ld[ec] = INC(
                    sync.dma_start(out=ab[:, A_W2Q + (ec % 2) * HIDP:
                                       A_W2Q + (ec % 2) * HIDP + HIDP],
                                   in_=d_w2q[ec]), "ld", 16)
        on("sync", p_fb_sync)



        def emit_fb_stt(ec):
            def vfun(vector, ec=ec):
                if ec == 0:
                    W(vector, "lc", LC_X1R)
                pT = pH[2 + ec % 2]
                W(vector, "bt", 2 * ec + 2)
                last = None
                for t in range(8):
                    x1s = aff(F_X1 + t * 2048 + ec * 128, 128)
                    last = vector.tensor_tensor(x1s, pT[:, t * 128:(t + 1) * 128],
                                                x1s, ALU.add)
                INC(last, "bs", 1)
            on("vector", vfun)

        def emit_fb_evac(ec):
            def sfun(scalar, ec=ec):
                par = ec % 2
                W(scalar, "bm", ec + 1)
                for half in range(2):
                    if half == 0:
                        if ec >= 1:
                            W(scalar, "bt", 2 * ec)
                    else:
                        W(scalar, "bt", 2 * ec + 1)
                    INC(scalar.activation(aff(F_FSC, 512),
                                          pH[par][:, half * 512:(half + 1) * 512],
                                          AF.Copy, scale=aff(F_NS + 2, 1)), "be", 1)
            on("scalar", sfun)

        for ec in range(NEC):
            if ec >= 1:
                emit_fb_evac(ec - 1)
            if ec >= 2:
                emit_fb_stt(ec - 2)

        def p_fb_tensor(tensor):
            W(tensor, "fg", FG_ALL)
            for ec in range(NEC + 1):
                if ec < NEC:
                    par = ec % 2
                    W(tensor, "ld", w2l_ld[min(ec + 1, NEC - 1)])
                    if ec >= 2:
                        W(tensor, "be", (ec - 1) * 2)
                    ps = pH[par]
                    last = None
                    for hc in range(NHC):
                        lhs = abf(A_W2Q + par * HIDP + hc * 128, 128)
                        for b2 in range(2):
                            last = tensor.matmul(
                                ps[:, b2 * 512:(b2 + 1) * 512], lhs,
                                abf(A_G + hc * 1024 + b2 * 512, 512),
                                start=(hc == 0), stop=(hc == NHC - 1))
                    INC(last, "bm", 1)
                if ec >= 1:
                    jc = ec - 1
                    pT = pH[2 + jc % 2]
                    if jc >= 2:
                        W(tensor, "bs", jc - 1)
                    for half in range(2):
                        hh = jc * 2 + half
                        W(tensor, "be", hh + 1)
                        last_t = None
                        for t4 in range(4):
                            t = half * 4 + t4
                            last_t = tensor.transpose(
                                pT[:, t * 128:(t + 1) * 128],
                                aff(F_FSC + t4 * 128, 128), identf)
                        INC(last_t, "bt", 1)
        on("tensor", p_fb_tensor)


        emit_fb_evac(NEC - 1)
        emit_fb_stt(NEC - 2)
        emit_fb_stt(NEC - 1)
        def p_fb_vdone(vector):
            W(vector, "bs", NEC)
            INC(vector.tensor_copy(aff(F_LN + 0, 1), aff(F_ONC, 1)), "fV", 1)
        on("vector", p_fb_vdone)
        FV_FB = cnt["fV"]

        if debug_stage == "x2pre":
            emit_store_out(FV_FB)
            ps_stack3.close()
            return nc

        emit_ln(FV_FB)
        FV_LN2 = cnt["fV"]
        emit_store_out(FV_LN2, per_tile=list(ln_tile_done))
        ps_stack3.close()

    return nc


def _host_shared(Wq, bq, Wk, bk, Wv, bv, w1, w3, w2):
    f32 = np.float32
    wq_sb = np.ascontiguousarray(
        Wq.reshape(HH, NEC, 128, DD).transpose(0, 2, 1, 3).reshape(HH, 128, 2048),
        dtype=f32)
    wk_sb = np.ascontiguousarray(
        Wk.reshape(GG, NEC, 128, DD).transpose(0, 2, 1, 3).reshape(GG, 128, 2048),
        dtype=f32)
    wv_cat = np.concatenate([Wv[g] for g in range(GG)], axis=1)
    wv_sb = np.ascontiguousarray(
        wv_cat.reshape(NEC, 128, 512).transpose(1, 0, 2).reshape(128, 8192), dtype=f32)

    w1T = np.zeros((EE, HIDP), dtype=f32)
    w1T[:, :HID] = w1.T
    w1_sb = np.ascontiguousarray(
        w1T.reshape(NEC, 128, NHC, 128).transpose(2, 1, 0, 3).reshape(NHC, 128, 2048),
        dtype=f32)
    w3T = np.zeros((EE, HIDP), dtype=f32)
    w3T[:, :HID] = w3.T
    w3_sb = np.ascontiguousarray(
        w3T.reshape(NEC, 128, NHC, 128).transpose(2, 1, 0, 3).reshape(NHC, 128, 2048),
        dtype=f32)
    w2T = np.zeros((HIDP, EE), dtype=f32)
    w2T[:HID, :] = w2.T
    w2_sb = np.ascontiguousarray(
        w2T.reshape(NHC, 128, NEC, 128).transpose(2, 1, 0, 3).reshape(NEC, 128, HIDP),
        dtype=f32)

    constf = np.zeros((128, 432), dtype=f32)
    constf[:, F_IDF:F_IDF + 128] = np.eye(128, dtype=f32)
    constf[:, F_BQT:F_BQT + HH] = bq.T
    constf[:, F_BKT:F_BKT + GG] = bk.T
    constf[0, F_ONR:F_ONR + 128] = 1.0
    constf[:, F_ONC] = 1.0
    constf[:, F_M16] = -16.0
    constf[:, F_EPS] = 1e-5

    bv_cat = np.ascontiguousarray(
        np.concatenate([bv[g] for g in range(GG)]).reshape(1, 512), dtype=f32)
    constb = np.zeros((128, 768), dtype=ml_dtypes.bfloat16)
    constb[:, 0:128] = np.eye(128, dtype=np.float32).astype(ml_dtypes.bfloat16)
    constb[:, 128:256] = np.float32(1.0)

    masks = []
    for v in range(2):
        mk = np.zeros((128, 2048), dtype=ml_dtypes.bfloat16)
        for kc in range(16):
            nv = NV[kc]
            jstar = nv - 1
            rstar = _r_of_slot(v, jstar)
            p = np.arange(128)[:, None]
            q = np.arange(128)[None, :]
            mk[:, kc * 128:(kc + 1) * 128] = (
                (kc * 128 + p) <= (rstar * 128 + q)).astype(ml_dtypes.bfloat16)
            for j in range(nv - 1):
                assert kc * 128 + 127 < _r_of_slot(v, j) * 128, (v, kc, j)
            for j in range(nv, 8):
                assert NK[j] <= kc
        masks.append(mk)

    return dict(wq_sb=wq_sb, wk_sb=wk_sb, wv_sb=wv_sb, w1_sb=w1_sb, w3_sb=w3_sb,
                w2_sb=w2_sb, constf=constf, constb=constb, bvf=bv_cat), masks


def kernel(L0_attn_x, L0_attn_Wq, L0_attn_bq, L0_attn_Wk, L0_attn_bk,
           L0_attn_Wv, L0_attn_bv, L0_ffn_w1, L0_ffn_w3, L0_ffn_w2):
    x = np.asarray(L0_attn_x, dtype=np.float32)
    shared, masks = _host_shared(
        np.asarray(L0_attn_Wq, dtype=np.float32),
        np.asarray(L0_attn_bq, dtype=np.float32),
        np.asarray(L0_attn_Wk, dtype=np.float32),
        np.asarray(L0_attn_bk, dtype=np.float32),
        np.asarray(L0_attn_Wv, dtype=np.float32),
        np.asarray(L0_attn_bv, dtype=np.float32),
        np.asarray(L0_ffn_w1, dtype=np.float32),
        np.asarray(L0_ffn_w3, dtype=np.float32),
        np.asarray(L0_ffn_w2, dtype=np.float32))

    in_maps = []
    rows_per_core = []
    for c in range(8):
        b, v = divmod(c, 2)
        rows = np.concatenate([np.arange(_r_of_slot(v, s) * 128,
                                         (_r_of_slot(v, s) + 1) * 128)
                               for s in range(8)])
        rows_per_core.append((b, rows))
        xb = x[b]
        m = dict(shared)
        m["xkvT"] = np.ascontiguousarray(
            xb.T.reshape(NEC, 128, SS).transpose(1, 0, 2).reshape(128, NEC * SS))
        m["xqT"] = np.ascontiguousarray(
            xb[rows].T.reshape(NEC, 128, 1024).transpose(1, 0, 2).reshape(128, NEC * 1024))
        m["xq"] = np.ascontiguousarray(xb[rows])
        m["masks"] = masks[v]
        in_maps.append(m)

    nc = _build_nc(DEBUG_STAGE)
    res = run_bass_kernel_spmd(nc, in_maps, core_ids=list(range(8)))

    y = np.empty((BB, SS, EE), dtype=np.float32)
    for c in range(8):
        b, rows = rows_per_core[c]
        y[b][rows] = res.results[c]["out"]
    return y
